# version 15
# baseline (speedup 1.0000x reference)
import numpy as np
import concourse.bass as bass
import concourse.mybir as mybir
from concourse.bass_utils import run_bass_kernel_spmd

F32 = mybir.dt.float32
BF16 = mybir.dt.bfloat16
AF = mybir.ActivationFunctionType
ALU = mybir.AluOpType
AX = mybir.AxisListType

NCORES = 8
TC = 2048
NB = 4
NT = 16
D = 1024
NDC = 8
H = 2816
NHC = 22
NG = 11
NEX = 9
NTT = 8
EPS_ATTN = 1e-6
EPS_LN = 1e-5

ENGINES = ("PE", "ACT", "DVE", "POOL", "SP")


class Sched:
    def __init__(self):
        self.q = {e: [] for e in ENGINES}
        self.cnt = {"pe": 0, "act": 0, "dve": 0}
        for base in ("sp", "wd"):
            for i in range(8):
                self.cnt[f"{base}{i}"] = 0
        self.dma_idx = {"sp": 0, "wd": 0}
        self.gates = {}

    def op(self, eng, fn, waits=(), inc=None):
        if inc is not None:
            self.cnt[inc[0]] += inc[1]
        self.q[eng].append((fn, tuple(waits), inc))
        return (inc[0], self.cnt[inc[0]]) if inc is not None else None

    def acq(self, names):
        w = []
        for n in names:
            w += self.gates.get(n, [])
        return w

    def rel(self, names, semcnt):
        lst = semcnt if isinstance(semcnt, list) else [semcnt]
        for n in names:
            self.gates[n] = list(lst)

    def emit(self, block, sems):
        binders = {"PE": block.tensor, "ACT": block.scalar, "DVE": block.vector,
                   "POOL": block.gpsimd, "SP": block.sync}
        for name in ENGINES:
            ops = self.q[name]

            def make(ops=ops):
                def run(e):
                    maxw = {}
                    for fn, waits, inc in ops:
                        for s, v in waits:
                            if v > maxw.get(s, 0):
                                e.wait_ge(sems[s], v)
                                maxw[s] = v
                        inst = fn(e)
                        if inc is not None:
                            inst.then_inc(sems[inc[0]], inc[1])
                return run
            binders[name](make())


def _mx(lst):
    return max(lst, key=lambda t: t[1])


def build_nc():
    nc = bass.Bass("TRN2", target_bir_lowering=False)
    S = Sched()

    x_d = nc.dram_tensor("x", [NB * TC, D], F32, kind="ExternalInput").ap()
    wq_d = nc.dram_tensor("wqP", [128, 8192], F32, kind="ExternalInput").ap()
    wk_d = nc.dram_tensor("wkP", [128, 8192], F32, kind="ExternalInput").ap()
    wv_d = nc.dram_tensor("wvP", [128, 8192], F32, kind="ExternalInput").ap()
    wout_d = nc.dram_tensor("woutP", [128, 8192], F32, kind="ExternalInput").ap()
    wg_d = nc.dram_tensor("wgT", [D, 8], F32, kind="ExternalInput").ap()
    b_d = nc.dram_tensor("b_out", [1, D], F32, kind="ExternalInput").ap()
    rb_d = nc.dram_tensor("rbias", [1, 8], F32, kind="ExternalInput").ap()
    w12_d = nc.dram_tensor("w12P", [NEX, NG, 128, 4096], F32, kind="ExternalInput").ap()
    w3_d = nc.dram_tensor("w3P", [NEX, 128, 22528], F32, kind="ExternalInput").ap()
    eye_d = nc.dram_tensor("eye", [128, 128], F32, kind="ExternalInput").ap()
    y_d = nc.dram_tensor("y", [TC, D], F32, kind="ExternalOutput").ap()
    x2scr_d = nc.dram_tensor("x2scr", [TC, D], F32, kind="Internal").ap()

    from contextlib import ExitStack
    with ExitStack() as stack:
        en = stack.enter_context
        AR = en(nc.sbuf_tensor("AR", [128, 73728], BF16))
        x2t = en(nc.sbuf_tensor("x2t", [128, 8200], F32))
        x_stage = en(nc.sbuf_tensor("x_stage", [128, 3072], F32))
        tmp_f32 = en(nc.sbuf_tensor("tmp_f32", [128, 1024], F32))
        b_bcast = en(nc.sbuf_tensor("b_bcast", [128, 1024], F32))
        rb_bcast = en(nc.sbuf_tensor("rb_bcast", [128, 8], F32))
        s1s2 = en(nc.sbuf_tensor("s1s2", [128, 32], F32))
        mvar = en(nc.sbuf_tensor("mvar", [128, 16], F32))
        rstd_sb = en(nc.sbuf_tensor("rstd_sb", [128, 16], F32))
        nb_sb = en(nc.sbuf_tensor("nb_sb", [128, 16], F32))
        z_sb = en(nc.sbuf_tensor("z_sb", [128, 16], F32))
        gate_sb = en(nc.sbuf_tensor("gate_sb", [128, 128], F32))
        gsc = en(nc.sbuf_tensor("gsc", [128, 8], F32))
        mcol = en(nc.sbuf_tensor("mcol", [128, 4], F32))
        negm = en(nc.sbuf_tensor("negm", [128, 1], F32))
        dencol = en(nc.sbuf_tensor("dencol", [128, 1], F32))
        rt_a = en(nc.sbuf_tensor("rt_a", [128, 8], F32))
        rt_b = en(nc.sbuf_tensor("rt_b", [128, 8], F32))
        wexp = en(nc.sbuf_tensor("wexp", [128, 128], F32))
        rb_row = en(nc.sbuf_tensor("rb_row", [1, 8], F32))
        ones_f = en(nc.sbuf_tensor("ones_f", [1, 128], F32))
        eps_col = en(nc.sbuf_tensor("eps_col", [128, 1], F32))
        xn_stage = en(nc.sbuf_tensor("xn_stage", [128, 4096], BF16))
        eye_b = en(nc.sbuf_tensor("eye_b", [128, 128], BF16))
        ones_b = en(nc.sbuf_tensor("ones_b", [128, 1], BF16))
        ones_row_b = en(nc.sbuf_tensor("ones_row_b", [1, 128], BF16))
        rb_row_b = en(nc.sbuf_tensor("rb_row_b", [1, 8], BF16))
        ksum_fm = en(nc.sbuf_tensor("ksum_fm", [128, 8], BF16))
        wg_sb = en(nc.sbuf_tensor("wg_sb", [128, 64], BF16))
        big1 = en(nc.psum_tensor("big1", [128, 1024], F32))
        big2 = en(nc.psum_tensor("big2", [128, 1024], F32))
        u1 = en(nc.psum_tensor("u1", [128, 512], F32))
        u2 = en(nc.psum_tensor("u2", [128, 512], F32))
        utr = en(nc.psum_tensor("utr", [128, 1024], F32))
        sem_pe = en(nc.semaphore("pe"))
        sem_act = en(nc.semaphore("act"))
        sem_dve = en(nc.semaphore("dve"))
        sems = {"pe": sem_pe, "act": sem_act, "dve": sem_dve}
        for base in ("sp", "wd"):
            for i in range(8):
                sems[f"{base}{i}"] = en(nc.semaphore(f"{base}{i}"))
        block = en(nc.Block())
        b_row = tmp_f32[0:1, :]

        xn_fm = AR[:, 0:16384]
        q_fm = AR[:, 16384:32768]
        k_tm = AR[:, 32768:49152]
        v_tm = AR[:, 49152:65536]
        wbuf = [AR[:, 65536:69632], AR[:, 69632:73728]]
        wbuf_names = ["wb0", "wb1"]
        attn_fm = AR[:, 0:16384]
        kv_sb = AR[:, 32768:40960]
        woutT_sb = AR[:, 40960:49152]
        su = AR[:, 0:22528]
        w3_sb = AR[:, 22528:45056]
        x2n = [AR[:, 49152:57344], AR[:, 57344:65536]]
        w12buf = wbuf

        psl = {"b1a": big1[:, 0:512], "b1b": big1[:, 512:1024],
               "b2a": big2[:, 0:512], "b2b": big2[:, 512:1024],
               "u1": u1[:, :], "u2": u2[:, :],
               "utr0": utr[:, 0:512], "utr1": utr[:, 512:1024]}

        def mm_group(out_ap, pairs, waits):
            n = len(pairs)
            for i, (l, r) in enumerate(pairs):
                S.op("PE",
                     (lambda e, o=out_ap, l=l, r=r, st=(i == 0), so=(i == n - 1):
                      e.matmul(o, l, r, start=st, stop=so, skip_group_check=True)),
                     waits=waits if i == 0 else (),
                     inc=("pe", 1) if i == n - 1 else None)
            return ("pe", S.cnt["pe"])

        def dma(eng, base, out_ap, in_ap, waits):
            i = S.dma_idx[base]
            S.dma_idx[base] = i + 1
            sem = f"{base}{i % 8}"
            return S.op(eng, (lambda e, o=out_ap, i=in_ap: e.dma_start(out=o, in_=i)),
                        waits=waits, inc=(sem, 16))

        c_eye = dma("POOL", "wd", eye_b[:, :], eye_d, [])
        c_brow = dma("SP", "sp", b_row, b_d, [])
        c_rbrow = dma("SP", "sp", rb_row[:, :], rb_d, [])
        wg3 = wg_d.rearrange("(c p) f -> p c f", p=128)
        c_wg = dma("POOL", "wd", wg_sb[:, :].rearrange("p (c f) -> p c f", c=8), wg3, [])
        c_ones = S.op("DVE", lambda e: e.memset(ones_f[:, :], 1.0), inc=("dve", 1))
        c_onesb = S.op("DVE", lambda e: e.memset(ones_b[:, :], 1.0), inc=("dve", 1))
        c_onesrb = S.op("DVE", lambda e: e.memset(ones_row_b[:, :], 1.0), inc=("dve", 1))
        c_eps = S.op("DVE", lambda e: e.memset(eps_col[:, :], EPS_LN), inc=("dve", 1))
        c_rbb = S.op("DVE", lambda e: e.tensor_copy(rb_row_b[:, :], rb_row[:, :]),
                     waits=[c_rbrow], inc=("dve", 1))

        g1 = mm_group(u1[:, :], [(ones_f[:, :], b_row[0:1, 0:512])], [c_brow, c_ones])
        g2 = mm_group(u2[:, :], [(ones_f[:, :], b_row[0:1, 512:1024])], [c_brow])
        S.rel(["tmp0", "tmp1"], g2)
        c_bb1 = S.op("DVE", lambda e: e.tensor_copy(b_bcast[:, 0:512], u1[:, :]),
                     waits=[g1], inc=("dve", 1))
        c_bb2 = S.op("DVE", lambda e: e.tensor_copy(b_bcast[:, 512:1024], u2[:, :]),
                     waits=[g2], inc=("dve", 1))
        g3 = mm_group(u1[:, 0:8], [(ones_f[:, :], rb_row[0:1, :])], [c_rbrow, c_bb1])
        c_rbbc = S.op("DVE", lambda e: e.tensor_copy(rb_bcast[:, :], u1[:, 0:8]),
                      waits=[g3], inc=("dve", 1))
        S.rel(["u1"], c_rbbc)
        S.rel(["u2"], c_bb2)
        ready_b = [c_bb1, c_bb2]
        ready_rbb = [c_rbbc]

        chain = {"ln_dve": None, "ln_act": None, "rt": None}

        def ln_chunk(src, xs_col, k, src_gate):
            mv = mvar[:, (xs_col % 4) * 4:(xs_col % 4) * 4 + 4]
            mv_gate = S.acq([f"mv{xs_col % 4}"])
            c_s1 = S.op("ACT", (lambda e, s=src, o=s1s2[:, xs_col:xs_col + 1]:
                                e.activation(tmp_f32[:, :], s, AF.Copy, accum_out=o)),
                        waits=src_gate + S.acq(["tmp0", "tmp1"]), inc=("act", 1))
            c_s2 = S.op("ACT", (lambda e, s=src, o=s1s2[:, 16 + xs_col:17 + xs_col]:
                                e.activation(tmp_f32[:, :], s, AF.Square, accum_out=o)),
                        waits=src_gate + [c_s1], inc=("act", 1))
            S.rel(["tmp0", "tmp1"], c_s2)
            c_m = S.op("DVE", (lambda e, i=s1s2[:, xs_col:xs_col + 1], o=mv[:, 0:1]:
                               e.tensor_scalar_mul(o, i, 1.0 / D)),
                       waits=[c_s1] + mv_gate, inc=("dve", 1))
            c_msq = S.op("DVE", (lambda e, o=mv[:, 1:2], i=mv[:, 0:1]:
                                 e.tensor_mul(o, i, i)),
                         waits=[c_m], inc=("dve", 1))
            c_var = S.op("DVE", (lambda e, i=s1s2[:, 16 + xs_col:17 + xs_col],
                                 o=mv[:, 2:3], m2=mv[:, 1:2]:
                                 e.scalar_tensor_tensor(o, i, 1.0 / D,
                                                        m2, ALU.mult, ALU.subtract)),
                         waits=[c_msq, c_s2], inc=("dve", 1))
            c_ln = S.op("ACT", (lambda e, o=mv[:, 3:4], i=mv[:, 2:3]:
                                e.activation(o, i, AF.Ln, bias=eps_col[:, :])),
                        waits=[c_var, c_eps], inc=("act", 1))
            c_rstd = S.op("ACT", (lambda e, o=rstd_sb[:, xs_col:xs_col + 1], i=mv[:, 3:4]:
                                  e.activation(o, i, AF.Exp, scale=-0.5)),
                          waits=[c_ln], inc=("act", 1))
            c_nb = S.op("DVE", (lambda e, o=nb_sb[:, xs_col:xs_col + 1], i=mv[:, 0:1]:
                                e.tensor_scalar_mul(o, i, -1.0)),
                        waits=[c_var], inc=("dve", 1))
            S.rel([f"mv{xs_col % 4}"], [c_rstd, c_nb])
            dst = xn_stage[:, k * 1024:(k + 1) * 1024]
            c_ap = S.op("DVE", (lambda e, s=src, d=dst,
                                r=rstd_sb[:, xs_col:xs_col + 1],
                                b=nb_sb[:, xs_col:xs_col + 1]:
                                e.tensor_scalar(d, s, b, r, ALU.add, ALU.mult)),
                        waits=[c_nb, c_rstd] + S.acq([f"xn{k}"]), inc=("dve", 1))
            return c_s1, c_ap

        def transpose_chunk(k, dst3_ap, c_apply, extra=()):
            gates = [c_apply, c_eye] + S.acq(["utr0", "utr1"]) + list(extra)
            g = None
            for c in range(8):
                src = xn_stage[:, k * 1024 + c * 128: k * 1024 + (c + 1) * 128]
                g = mm_group(utr[:, c * 128:(c + 1) * 128], [(src, eye_b[:, :])],
                             gates if c == 0 else [])
            c_cp = S.op("DVE", (lambda e, o=dst3_ap:
                                e.tensor_copy(o, utr[:, :].rearrange("p (c f) -> p c f", c=8))),
                        waits=[g], inc=("dve", 1))
            S.rel(["utr0", "utr1"], c_cp)
            return c_cp

        slots6 = ["u1", "u2", "b1a", "b1b", "b2a", "b2b"]
        sidx = [0]

        def tm_mms(dst_tm, tc, half, bslot, gates):
            sl = slots6[sidx[0] % 6]; sidx[0] += 1
            ps = psl[sl]
            pairs = [(xn_fm[:, c * 2048 + tc * 128: c * 2048 + (tc + 1) * 128],
                      wbuf[bslot][:, c * 512:(c + 1) * 512])
                     for c in range(8)]
            g = mm_group(ps, pairs, gates + S.acq([sl]))
            dst = dst_tm[:, tc * 1024 + half * 512: tc * 1024 + half * 512 + 512]

            def emit_evac(is_k):
                if is_k:
                    c_ev = S.op("DVE", (lambda e, o=dst, i_=ps:
                                        e.tensor_scalar(o, i_, 0.0, EPS_ATTN,
                                                        ALU.max, ALU.add)),
                                waits=[g] + S.acq(["ktm"]), inc=("dve", 1))
                else:
                    c_ev = S.op("ACT", (lambda e, o=dst, i_=ps:
                                        e.activation(o, i_, AF.Copy)),
                                waits=[g] + S.acq(["vtm"]), inc=("act", 1))
                S.rel([sl], c_ev)
                return c_ev
            return emit_evac

        kv_mm_last = None
        for b in range(NB):
            own = (b == NB - 1)
            c_k0 = dma("POOL", "wd", wbuf[0], wk_d[:, 0:4096], S.acq(["wb0"]))
            c_k1 = dma("POOL", "wd", wbuf[1], wk_d[:, 4096:8192], S.acq(["wb1"]))
            ln_last = [None] * NT
            k_ev, v_ev = [], []
            pend_ev = None
            for tc in range(NT):
                k = tc % 4
                xs = x_stage[:, (tc % 3) * 1024:(tc % 3 + 1) * 1024]
                c_x = dma("SP", "sp", xs,
                          x_d[b * TC + tc * 128: b * TC + (tc + 1) * 128, :],
                          S.acq([f"xs{tc % 3}"]))
                c_s2, c_ap = ln_chunk(xs, tc, k, [c_x])
                S.rel([f"xs{tc % 3}"], [c_ap, c_s2])
                if pend_ev is not None:
                    k_ev += [f(True) for f in pend_ev]
                dst3 = xn_fm.rearrange("p (c t) -> p c t", c=8)[:, :, tc * 128:(tc + 1) * 128]
                c_last = transpose_chunk(k, dst3, c_ap, extra=S.acq(["xnfm"]))
                S.rel([f"xn{k}"], c_last)
                ln_last[tc] = c_last
                pend_ev = [tm_mms(k_tm, tc, 0, 0, [c_k0, c_last]),
                           tm_mms(k_tm, tc, 1, 1, [c_k1, c_last])]
            k_ev += [f(True) for f in pend_ev]
            pe_sweep1 = ("pe", S.cnt["pe"])
            c_v0 = dma("POOL", "wd", wbuf[0], wv_d[:, 0:4096], [pe_sweep1])
            c_v1 = dma("POOL", "wd", wbuf[1], wv_d[:, 4096:8192], [pe_sweep1])
            pend_ev = None
            for tc in range(NT):
                if pend_ev is not None:
                    v_ev += [f(False) for f in pend_ev]
                pend_ev = [tm_mms(v_tm, tc, 0, 0, [c_v0, ln_last[tc]]),
                           tm_mms(v_tm, tc, 1, 1, [c_v1, ln_last[tc]])]
            v_ev += [f(False) for f in pend_ev]
            S.rel(["xnfm"] if not own else [], ("pe", S.cnt["pe"]))
            pe_sweep2 = ("pe", S.cnt["pe"])
            S.rel(["wb0", "wb1"], pe_sweep2)

            kready = [_mx(k_ev), _mx(v_ev), c_onesb]
            for dc in range(NDC):
                bign = ["b1a", "b1b"] if dc % 2 == 0 else ["b2a", "b2b"]
                bigps = big1 if dc % 2 == 0 else big2
                g_eh = []
                for eh in range(2):
                    pairs = [(k_tm[:, tc * 1024 + dc * 128: tc * 1024 + (dc + 1) * 128],
                              v_tm[:, tc * 1024 + eh * 512: tc * 1024 + (eh + 1) * 512])
                             for tc in range(NT)]
                    g_eh.append(mm_group(bigps[:, eh * 512:(eh + 1) * 512], pairs,
                                         kready + S.acq([bign[eh]])))
                usl = "u1" if dc % 2 == 0 else "u2"
                ups = psl[usl]
                pairs = [(k_tm[:, tc * 1024 + dc * 128: tc * 1024 + (dc + 1) * 128],
                          ones_b[:, :]) for tc in range(NT)]
                g_ks = mm_group(ups[:, 0:1], pairs, kready + S.acq([usl]))
                kvslice = x2t[:, dc * 1025: dc * 1025 + 1024]
                ksslice = x2t[:, dc * 1025 + 1024: dc * 1025 + 1025]
                if b == 0:
                    c_e1 = S.op("DVE", (lambda e, o=kvslice, i_=bigps[:, :]:
                                        e.tensor_copy(o, i_)),
                                waits=g_eh, inc=("dve", 1))
                    c_e2 = S.op("DVE", (lambda e, o=ksslice, i_=ups[:, 0:1]:
                                        e.tensor_copy(o, i_)),
                                waits=[g_ks], inc=("dve", 1))
                else:
                    c_e1 = S.op("DVE", (lambda e, o=kvslice, i_=bigps[:, :]:
                                        e.tensor_add(o, o, i_)),
                                waits=g_eh, inc=("dve", 1))
                    c_e2 = S.op("DVE", (lambda e, o=ksslice, i_=ups[:, 0:1]:
                                        e.tensor_add(o, o, i_)),
                                waits=[g_ks], inc=("dve", 1))
                S.rel(bign, c_e1)
                S.rel([usl], c_e2)
                kv_mm_last = ("pe", S.cnt["pe"])
            S.rel(["ktm", "vtm"], kv_mm_last)
        kv_done_pe = ("pe", S.cnt["pe"])

        c_q0 = dma("POOL", "wd", wbuf[0], wq_d[:, 0:4096], S.acq(["wb0"]))
        c_q1 = dma("POOL", "wd", wbuf[1], wq_d[:, 4096:8192], S.acq(["wb1"]))
        q_ev = []
        for pname, c_pan in (("q0", c_q0), ("q1", c_q1)):
            bslot = 0 if pname == "q0" else 1
            for fq_rel in range(4):
                fq = (0 if pname == "q0" else 4) + fq_rel
                for ts in range(4):
                    sl = slots6[sidx[0] % 6]; sidx[0] += 1
                    ps = psl[sl]
                    pairs = [(wbuf[bslot][:, c * 512 + fq_rel * 128: c * 512 + (fq_rel + 1) * 128],
                              xn_fm[:, c * 2048 + ts * 512: c * 2048 + (ts + 1) * 512])
                             for c in range(8)]
                    g = mm_group(ps, pairs, [c_pan, _mx(ln_last)] + S.acq([sl]))
                    dstq = q_fm[:, fq * 2048 + ts * 512: fq * 2048 + (ts + 1) * 512]
                    c_ev = S.op("DVE", (lambda e, o=dstq, i_=ps:
                                        e.tensor_scalar(o, i_, 0.0, EPS_ATTN,
                                                        ALU.max, ALU.add)),
                                waits=[g], inc=("dve", 1))
                    S.rel([sl], c_ev)
                    q_ev.append(c_ev)
        ready_q = [_mx(q_ev)]
        pe_after_q = ("pe", S.cnt["pe"])
        S.rel(["wb0", "wb1"], pe_after_q)
        S.rel(["xnfm"], pe_after_q)

        c_cast = S.op("DVE",
                      (lambda e: e.tensor_copy(
                          kv_sb.rearrange("p (c f) -> p c f", c=8),
                          x2t[:, 0:8200].rearrange("p (c f) -> p c f", f=1025)[:, :, 0:1024])),
                      waits=[kv_done_pe], inc=("dve", 1))
        c_kscast = S.op("DVE",
                        (lambda e: e.tensor_copy(
                            ksum_fm[:, :].rearrange("p (c f) -> p c f", f=1),
                            x2t[:, 0:8200].rearrange("p (c f) -> p c f", f=1025)[:, :, 1024:1025])),
                        waits=[kv_done_pe], inc=("dve", 1))
        S.rel(["x2t"], [c_cast, c_kscast])
        c_wout = dma("POOL", "wd", woutT_sb, wout_d, [kv_done_pe])

        z_parts = []
        for tc in range(NT):
            usl = "u1" if tc % 2 == 0 else "u2"
            ups = psl[usl]
            pairs = [(q_fm[:, c * 2048 + tc * 128: c * 2048 + (tc + 1) * 128],
                      ksum_fm[:, c:c + 1])
                     for c in range(8)]
            g = mm_group(ups[:, 0:1], pairs, [c_kscast] + ready_q + S.acq([usl]))
            c_z = S.op("DVE", (lambda e, o=z_sb[:, tc:tc + 1], i_=ups[:, 0:1]:
                               e.tensor_scalar_add(o, i_, EPS_ATTN)),
                       waits=[g], inc=("dve", 1))
            S.rel([usl], c_z)
            z_parts.append(c_z)
        c_zr = S.op("DVE", lambda e: e.reciprocal(z_sb[:, :], z_sb[:, :]),
                    waits=[_mx(z_parts)], inc=("dve", 1))
        ready_z = [c_zr]

        attn_ev = []
        slots4 = ["b1a", "b1b", "b2a", "b2b"]
        sj = 0
        for ec in range(8):
            for ts in range(4):
                sl = slots4[sj % 4]; sj += 1
                ps = psl[sl]
                pairs = [(kv_sb[:, c * 1024 + ec * 128: c * 1024 + (ec + 1) * 128],
                          q_fm[:, c * 2048 + ts * 512: c * 2048 + (ts + 1) * 512])
                         for c in range(8)]
                g = mm_group(ps, pairs, [c_cast] + ready_q + S.acq([sl]))
                dst = attn_fm[:, ec * 2048 + ts * 512: ec * 2048 + (ts + 1) * 512]
                c_ev = S.op("ACT", (lambda e, o=dst, i_=ps: e.activation(o, i_, AF.Copy)),
                            waits=[g], inc=("act", 1))
                S.rel([sl], c_ev)
                attn_ev.append(c_ev)
        ready_attn = [_mx(attn_ev)]

        x2_store = {}
        t_fill = [dict(), dict()]
        ln2_last = [[], []]
        pend_tr = []

        def flush_tr(keep=0):
            while len(pend_tr) > keep:
                k, dst3, c_ap, tT = pend_tr.pop(0)
                c_last = transpose_chunk(k, dst3, c_ap, extra=[kv_done_pe])
                S.rel([f"xn{k}"], c_last)
                ln2_last[tT].append(c_last)
        for tc in range(NT):
            T, tc8 = tc // 8, tc % 8
            k = tc % 4
            xs = x_stage[:, (tc % 3) * 1024:(tc % 3 + 1) * 1024]
            c_x = dma("SP", "sp", xs,
                      x_d[(NB - 1) * TC + tc * 128: (NB - 1) * TC + (tc + 1) * 128, :],
                      S.acq([f"xs{tc % 3}"]))
            c_add = S.op("DVE", (lambda e, o=xs: e.tensor_add(o, o, b_bcast[:, :])),
                         waits=[c_x] + ready_b, inc=("dve", 1))
            bign = ["b1a", "b1b"] if tc % 2 == 0 else ["b2a", "b2b"]
            bigps = big1 if tc % 2 == 0 else big2
            gs = []
            for fh in range(2):
                pairs = [(attn_fm[:, c * 2048 + tc * 128: c * 2048 + (tc + 1) * 128],
                          woutT_sb[:, c * 1024 + fh * 512: c * 1024 + (fh + 1) * 512])
                         for c in range(8)]
                gs.append(mm_group(bigps[:, fh * 512:(fh + 1) * 512], pairs,
                                   [c_wout] + ready_attn + S.acq([bign[fh]])))
            if tc < 8:
                dst = x2t[:, tc * 1024:(tc + 1) * 1024]
                c_ev = S.op("DVE", (lambda e, o=dst, i_=bigps[:, :], z=z_sb[:, tc:tc + 1],
                                    xi=xs: e.scalar_tensor_tensor(o, i_, z, xi,
                                                                  ALU.mult, ALU.add)),
                            waits=gs + [c_add] + ready_z + S.acq(["x2t"]), inc=("dve", 1))
                S.rel(bign, c_ev)
                S.rel([f"xs{tc % 3}"], c_ev)
                t_fill[0][tc8] = c_ev
                src_ln = dst
                ln_gate = [c_ev]
            else:
                c_ev = S.op("DVE", (lambda e, o=xs, i_=bigps[:, :], z=z_sb[:, tc:tc + 1]:
                                    e.scalar_tensor_tensor(o, i_, z, o, ALU.mult, ALU.add)),
                            waits=gs + [c_add] + ready_z, inc=("dve", 1))
                S.rel(bign, c_ev)
                src_ln = xs
                ln_gate = [c_ev]
            c_s2, c_ap = ln_chunk(src_ln, tc, k, ln_gate)
            if tc >= 8:
                c_st = dma("SP", "sp", x2scr_d[tc8 * 128:(tc8 + 1) * 128, :], xs, [c_ev])
                S.rel([f"xs{tc % 3}"], [c_st, c_ap, c_s2])
                x2_store[tc8] = c_st
            flush_tr(keep=1)
            pend_tr.append((k,
                            x2n[T].rearrange("p (c t) -> p c t", c=8)[:, :, tc8 * 128:(tc8 + 1) * 128],
                            c_ap, T))
        flush_tr()
        pe_after_a4 = ("pe", S.cnt["pe"])
        S.rel(["su_war"], pe_after_a4)
        S.rel(["w3"], pe_after_a4)
        ready_x2n_t = [[_mx(ln2_last[0])], [_mx(ln2_last[1])]]

        ready_wexp_t = [None, None]
        for gtc in range(16):
            T, tc8 = gtc // 8, gtc % 8
            usl = "u1" if gtc % 2 == 0 else "u2"
            ups = psl[usl]
            pairs = [(x2n[T][:, c * 1024 + tc8 * 128: c * 1024 + (tc8 + 1) * 128],
                      wg_sb[:, c * 8:(c + 1) * 8]) for c in range(8)]
            g = mm_group(ups[:, 0:8], pairs,
                         [c_wg] + ready_x2n_t[T] + S.acq([usl]))
            graw = gate_sb[:, gtc * 8:(gtc + 1) * 8]
            c_g = S.op("ACT", (lambda e, o=graw, i_=ups[:, 0:8]:
                               e.activation(o, i_, AF.Copy)),
                       waits=[g], inc=("act", 1))
            S.rel([usl], c_g)
            gw = gsc[:, 0:8]
            gcol = gsc[:, 64 + (gtc % 8) * 8: 64 + (gtc % 8) * 8 + 8] if False else None
            prev_rt = [chain["rt"]] if chain["rt"] else []
            gcol = gate_sb[:, gtc * 8:(gtc + 1) * 8]
            c = S.op("DVE", (lambda e, o=gcol, i_=graw: e.tensor_add(o, i_, rb_bcast[:, :])),
                     waits=[c_g] + ready_rbb + prev_rt, inc=("dve", 1))
            c = S.op("DVE", (lambda e, g_=gcol: e.reduce_max(mcol[:, 0:1], g_, AX.X)),
                     waits=[c], inc=("dve", 1))
            c = S.op("DVE", (lambda e, g_=gcol: e.tensor_single_scalar(
                rt_a[:, :], g_, mcol[:, 0:1], ALU.is_equal)), waits=[c], inc=("dve", 1))
            c = S.op("DVE", (lambda e, g_=gcol, o=gw: e.scalar_tensor_tensor(
                o, rt_a[:, :], -1e30, g_, ALU.mult, ALU.add)), waits=[c], inc=("dve", 1))
            for step in (1, 2):
                c = S.op("DVE", (lambda e, o=mcol[:, step:step + 1], i_=gw:
                                 e.reduce_max(o, i_, AX.X)), waits=[c], inc=("dve", 1))
                c = S.op("DVE", (lambda e, i_=gw, m=mcol[:, step:step + 1]:
                                 e.tensor_single_scalar(rt_a[:, :], i_, m, ALU.is_equal)),
                         waits=[c], inc=("dve", 1))
                c = S.op("DVE", (lambda e, o=gw: e.scalar_tensor_tensor(
                    o, rt_a[:, :], -1e30, o, ALU.mult, ALU.add)), waits=[c], inc=("dve", 1))
            c = S.op("DVE", (lambda e, i_=gw: e.reduce_max(mcol[:, 3:4], i_, AX.X)),
                     waits=[c], inc=("dve", 1))
            c = S.op("DVE", lambda e: e.tensor_scalar_mul(negm[:, :], mcol[:, 0:1], -1.0),
                     waits=[c], inc=("dve", 1))
            c_exp = S.op("ACT", (lambda e, g_=gcol: e.activation(
                rt_b[:, :], g_, AF.Exp, bias=negm[:, :])), waits=[c], inc=("act", 1))
            c = S.op("DVE", (lambda e, g_=gcol: e.tensor_single_scalar(
                rt_a[:, :], g_, mcol[:, 3:4], ALU.is_ge)), waits=[c_exp], inc=("dve", 1))
            c = S.op("DVE", lambda e: e.tensor_mul(rt_b[:, :], rt_b[:, :], rt_a[:, :]),
                     waits=[c], inc=("dve", 1))
            c = S.op("DVE", lambda e: e.reduce_sum(dencol[:, :], rt_b[:, :], AX.X),
                     waits=[c], inc=("dve", 1))
            c = S.op("DVE", lambda e: e.reciprocal(dencol[:, :], dencol[:, :]),
                     waits=[c], inc=("dve", 1))
            wcol = wexp[:, gtc * 8:(gtc + 1) * 8]
            c_w = S.op("DVE", (lambda e, o=wcol: e.tensor_scalar_mul(
                o, rt_b[:, :], dencol[:, :])), waits=[c], inc=("dve", 1))
            chain["rt"] = c_w
            if tc8 == 7:
                ready_wexp_t[T] = [c_w]

        y_slice_store = {}
        for T in range(2):
            ready_x2n = ready_x2n_t[T]
            ready_wexp = ready_wexp_t[T]
            x2nT = x2n[T]
            if T == 1:
                for ts8 in range(NTT):
                    c_l = dma("SP", "sp",
                              x2t[:, ts8 * 1024:(ts8 + 1) * 1024],
                              x2scr_d[ts8 * 128:(ts8 + 1) * 128, :],
                              [y_slice_store[ts8], x2_store[ts8]])
                    t_fill[1][ts8] = c_l
            eo_last_by_slice = {}
            for ex in range(NEX):
                c_g12 = {}

                def load_w12(g_, ex=ex):
                    bslot = g_ % 2
                    return dma("POOL", "wd", w12buf[bslot], w12_d[ex, g_],
                               S.acq([wbuf_names[bslot]]))

                c_g12[0] = load_w12(0)
                c_g12[1] = load_w12(1)
                c_w3_all = dma("POOL", "wd", w3_sb, w3_d[ex], S.acq(["w3"]))

                su_ev = []
                ui = 0
                for g_ in range(NG):
                    bslot = g_ % 2
                    if g_ not in c_g12:
                        c_g12[g_] = load_w12(g_)
                    for jj in (0, 1):
                        j = 2 * g_ + jj
                        for ts in range(2):
                            names = [("u1", "u2"), ("b1a", "b1b"),
                                     ("b2a", "b2b"), ("utr0", "utr1")][ui % 4]
                            ui += 1
                            ps_h1, ps_h2 = psl[names[0]], psl[names[1]]
                            w1l = [w12buf[bslot][:, c * 512 + jj * 128: c * 512 + (jj + 1) * 128]
                                   for c in range(8)]
                            w2l = [w12buf[bslot][:, c * 512 + 256 + jj * 128:
                                                 c * 512 + 256 + (jj + 1) * 128]
                                   for c in range(8)]
                            rhs = [x2nT[:, c * 1024 + ts * 512: c * 1024 + (ts + 1) * 512]
                                   for c in range(8)]
                            gw1 = [c_g12[g_]] + ready_x2n + S.acq([names[0]])
                            g1_ = mm_group(ps_h1, list(zip(w1l, rhs)), gw1)
                            g2_ = mm_group(ps_h2, list(zip(w2l, rhs)), S.acq([names[1]]))
                            dst_su = su[:, j * 1024 + ts * 512: j * 1024 + (ts + 1) * 512]
                            c_si = S.op("ACT", (lambda e, o=dst_su, i_=ps_h1:
                                                e.activation(o, i_, AF.Silu)),
                                        waits=[g1_] + S.acq(["su_war"]), inc=("act", 1))
                            c_su = S.op("DVE", (lambda e, o=dst_su, i_=ps_h2:
                                                e.tensor_mul(o, o, i_)),
                                        waits=[c_si, g2_], inc=("dve", 1))
                            S.rel([names[0]], c_si)
                            S.rel([names[1]], c_su)
                            su_ev.append(c_su)
                    S.rel([wbuf_names[bslot]], ("pe", S.cnt["pe"]))
                ready_su = [_mx(su_ev)]

                for ts8 in range(NTT):
                    bign = ["b2a", "b2b"] if ts8 % 2 == 0 else ["b1a", "b1b"]
                    bigps = big2 if ts8 % 2 == 0 else big1
                    first_w = [c_w3_all] + ready_su + S.acq(bign)
                    n_mm = 0
                    for j in range(NHC):
                        for dh in range(2):
                            st = (j == 0)
                            so = (j == NHC - 1)
                            S.op("PE",
                                 (lambda e, o=bigps[:, dh * 512:(dh + 1) * 512],
                                  l=su[:, j * 1024 + ts8 * 128: j * 1024 + (ts8 + 1) * 128],
                                  r=w3_sb[:, j * 1024 + dh * 512: j * 1024 + (dh + 1) * 512],
                                  st=st, so=so:
                                  e.matmul(o, l, r, start=st, stop=so, skip_group_check=True)),
                                 waits=first_w if n_mm == 0 else (),
                                 inc=("pe", 1) if (so and dh == 1) else None)
                            n_mm += 1
                    g_eo = ("pe", S.cnt["pe"])
                    dst = x2t[:, ts8 * 1024:(ts8 + 1) * 1024]
                    fillw = [t_fill[T][ts8]]
                    if ex == 0:
                        c_ev = S.op("DVE", (lambda e, o=dst, i_=bigps[:, :]:
                                            e.scalar_tensor_tensor(o, i_, 1.0, o,
                                                                   ALU.mult, ALU.add)),
                                    waits=[g_eo] + fillw, inc=("dve", 1))
                    else:
                        wcol = wexp[:, (T * 8 + ts8) * 8 + (ex - 1):
                                    (T * 8 + ts8) * 8 + ex]
                        c_ev = S.op("DVE", (lambda e, o=dst, i_=bigps[:, :], w=wcol:
                                            e.scalar_tensor_tensor(o, i_, w, o,
                                                                   ALU.mult, ALU.add)),
                                    waits=[g_eo] + ready_wexp + fillw, inc=("dve", 1))
                    S.rel(bign, c_ev)
                    eo_last_by_slice[ts8] = c_ev
                pe_now = ("pe", S.cnt["pe"])
                S.rel(["su_war"], pe_now)
                S.rel(["w3"], pe_now)

            for ts8 in range(NTT):
                c_y = dma("SP", "sp",
                          y_d[T * 1024 + ts8 * 128: T * 1024 + (ts8 + 1) * 128, :],
                          x2t[:, ts8 * 1024:(ts8 + 1) * 1024],
                          [eo_last_by_slice[ts8]])
                y_slice_store[ts8] = c_y

        S.op("SP", lambda e: e.nop(),
             waits=[(f"sp{i}", S.cnt[f"sp{i}"]) for i in range(8)
                    if S.cnt[f"sp{i}"] > 0])
        S.emit(block, sems)
    return nc, {e: len(S.q[e]) for e in ENGINES}


_NC_CACHE = {}


def _get_nc():
    if "nc" not in _NC_CACHE:
        _NC_CACHE["nc"] = build_nc()
    return _NC_CACHE["nc"]


def kernel(L0_attn_x, L0_attn_W_qkv, L0_attn_W_out, L0_attn_b_out,
           L0_ffn_Wg, L0_ffn_routing_bias, L0_ffn_W1, L0_ffn_W2, L0_ffn_W3,
           L0_ffn_Ws1, L0_ffn_Ws2, L0_ffn_Ws3):
    x = np.asarray(L0_attn_x, dtype=np.float32)
    B, N, D_ = x.shape
    x_flat = x.reshape(B * N, D_)

    wqkvT = np.asarray(L0_attn_W_qkv, np.float32).T

    def _pack_halves(blk):
        return np.ascontiguousarray(
            blk.reshape(8, 128, 2, 512).transpose(2, 1, 0, 3).reshape(2, 128, 4096)
            .transpose(1, 0, 2).reshape(128, 8192))

    wqP = _pack_halves(wqkvT[:, 0:1024])
    wkP = _pack_halves(wqkvT[:, 1024:2048])
    wvP = _pack_halves(wqkvT[:, 2048:3072])
    woutT = np.asarray(L0_attn_W_out, np.float32).T
    woutP = np.ascontiguousarray(
        woutT.reshape(8, 128, 1024).transpose(1, 0, 2).reshape(128, 8192))
    wgT = np.ascontiguousarray(np.asarray(L0_ffn_Wg, np.float32).T)
    b_out = np.ascontiguousarray(np.asarray(L0_attn_b_out, np.float32).reshape(1, D_))
    rb = np.ascontiguousarray(np.asarray(L0_ffn_routing_bias, np.float32).reshape(1, 8))

    W1 = np.asarray(L0_ffn_W1, np.float32)
    W2 = np.asarray(L0_ffn_W2, np.float32)
    W3 = np.asarray(L0_ffn_W3, np.float32)
    Ws1 = np.asarray(L0_ffn_Ws1, np.float32)
    Ws2 = np.asarray(L0_ffn_Ws2, np.float32)
    Ws3 = np.asarray(L0_ffn_Ws3, np.float32)

    Hr = W1.shape[1]
    W1T = np.zeros((NEX, D_, H), np.float32)
    W2T = np.zeros((NEX, D_, H), np.float32)
    W1T[0, :, :Hr] = Ws1.T
    W2T[0, :, :Hr] = Ws2.T
    W1T[1:, :, :Hr] = W1.transpose(0, 2, 1)
    W2T[1:, :, :Hr] = W2.transpose(0, 2, 1)
    w12 = np.empty((NEX, D_, NG, 512), np.float32)
    for g in range(NG):
        j0, j1 = 2 * g * 128, (2 * g + 1) * 128
        w12[:, :, g, 0:128] = W1T[:, :, j0:j0 + 128]
        w12[:, :, g, 128:256] = W1T[:, :, j1:j1 + 128]
        w12[:, :, g, 256:384] = W2T[:, :, j0:j0 + 128]
        w12[:, :, g, 384:512] = W2T[:, :, j1:j1 + 128]
    w12 = np.ascontiguousarray(
        w12.reshape(NEX, 8, 128, NG, 512).transpose(0, 3, 2, 1, 4).reshape(NEX, NG, 128, 4096))

    w3T = np.zeros((NEX, H, D_), np.float32)
    w3T[0, :Hr, :] = Ws3.T
    w3T[1:, :Hr, :] = W3.transpose(0, 2, 1)
    w3T = np.ascontiguousarray(
        w3T.reshape(NEX, 22, 128, 1024).transpose(0, 2, 1, 3).reshape(NEX, 128, 22528))

    eye = np.eye(128, dtype=np.float32)

    nc, _counts = _get_nc()
    in_maps = []
    for c in range(NCORES):
        g, r = c // 4, c % 4
        batch = x_flat[g * NB * TC:(g + 1) * NB * TC]
        order = [(r + j) % 4 for j in range(1, 4)] + [r]
        xb = np.ascontiguousarray(
            np.concatenate([batch[blk * TC:(blk + 1) * TC] for blk in order], axis=0))
        in_maps.append({
            "x": xb,
            "wqP": wqP, "wkP": wkP, "wvP": wvP, "woutP": woutP, "wgT": wgT,
            "b_out": b_out, "rbias": rb,
            "w12P": w12, "w3P": w3T, "eye": eye,
        })
    res = run_bass_kernel_spmd(nc, in_maps, core_ids=list(range(NCORES)))
    out = np.concatenate([np.asarray(r["y"]) for r in res.results], axis=0)
    return out.reshape(B, N, D_).astype(np.float32)


if __name__ == "__main__":
    import time
    t0 = time.time()
    nc, counts = build_nc()
    print(f"build {time.time()-t0:.1f}s, instruction queue sizes:", counts,
          "total:", sum(counts.values()))


# version 16 (final)
# speedup vs baseline: 1.0210x; 1.0210x over previous
import numpy as np
import concourse.bass as bass
import concourse.mybir as mybir
from concourse.bass_utils import run_bass_kernel_spmd

F32 = mybir.dt.float32
BF16 = mybir.dt.bfloat16
AF = mybir.ActivationFunctionType
ALU = mybir.AluOpType
AX = mybir.AxisListType

NCORES = 8
TC = 2048
NB = 4
NT = 16
D = 1024
NDC = 8
H = 2816
NHC = 22
NG = 11
NEX = 9
NTT = 8
EPS_ATTN = 1e-6
EPS_LN = 1e-5

ENGINES = ("PE", "ACT", "DVE", "POOL", "SP")


class Sched:
    def __init__(self):
        self.q = {e: [] for e in ENGINES}
        self.cnt = {"pe": 0, "act": 0, "dve": 0}
        for base in ("sp", "wd"):
            for i in range(8):
                self.cnt[f"{base}{i}"] = 0
        self.dma_idx = {"sp": 0, "wd": 0}
        self.gates = {}

    def op(self, eng, fn, waits=(), inc=None):
        if inc is not None:
            self.cnt[inc[0]] += inc[1]
        self.q[eng].append((fn, tuple(waits), inc))
        return (inc[0], self.cnt[inc[0]]) if inc is not None else None

    def acq(self, names):
        w = []
        for n in names:
            w += self.gates.get(n, [])
        return w

    def rel(self, names, semcnt):
        lst = semcnt if isinstance(semcnt, list) else [semcnt]
        for n in names:
            self.gates[n] = list(lst)

    def emit(self, block, sems):
        binders = {"PE": block.tensor, "ACT": block.scalar, "DVE": block.vector,
                   "POOL": block.gpsimd, "SP": block.sync}
        for name in ENGINES:
            ops = self.q[name]

            def make(ops=ops):
                def run(e):
                    maxw = {}
                    for fn, waits, inc in ops:
                        for s, v in waits:
                            if v > maxw.get(s, 0):
                                e.wait_ge(sems[s], v)
                                maxw[s] = v
                        inst = fn(e)
                        if inc is not None:
                            inst.then_inc(sems[inc[0]], inc[1])
                return run
            binders[name](make())


def _mx(lst):
    return max(lst, key=lambda t: t[1])


def build_nc():
    nc = bass.Bass("TRN2", target_bir_lowering=False)
    S = Sched()

    x_d = nc.dram_tensor("x", [NB * TC, D], F32, kind="ExternalInput").ap()
    wq_d = nc.dram_tensor("wqP", [128, 8192], F32, kind="ExternalInput").ap()
    wk_d = nc.dram_tensor("wkP", [128, 8192], F32, kind="ExternalInput").ap()
    wv_d = nc.dram_tensor("wvP", [128, 8192], F32, kind="ExternalInput").ap()
    wout_d = nc.dram_tensor("woutP", [128, 8192], F32, kind="ExternalInput").ap()
    wg_d = nc.dram_tensor("wgT", [D, 8], F32, kind="ExternalInput").ap()
    b_d = nc.dram_tensor("b_out", [1, D], F32, kind="ExternalInput").ap()
    rb_d = nc.dram_tensor("rbias", [1, 8], F32, kind="ExternalInput").ap()
    w12_d = nc.dram_tensor("w12P", [NEX, NG, 128, 4096], F32, kind="ExternalInput").ap()
    w3_d = nc.dram_tensor("w3P", [NEX, 128, 22528], F32, kind="ExternalInput").ap()
    eye_d = nc.dram_tensor("eye", [128, 128], F32, kind="ExternalInput").ap()
    y_d = nc.dram_tensor("y", [TC, D], F32, kind="ExternalOutput").ap()
    x2scr_d = nc.dram_tensor("x2scr", [TC, D], F32, kind="Internal").ap()

    from contextlib import ExitStack
    with ExitStack() as stack:
        en = stack.enter_context
        AR = en(nc.sbuf_tensor("AR", [128, 73728], BF16))
        x2t = en(nc.sbuf_tensor("x2t", [128, 8200], F32))
        x_stage = en(nc.sbuf_tensor("x_stage", [128, 3072], F32))
        tmp_f32 = en(nc.sbuf_tensor("tmp_f32", [128, 1024], F32))
        b_bcast = en(nc.sbuf_tensor("b_bcast", [128, 1024], F32))
        rb_bcast = en(nc.sbuf_tensor("rb_bcast", [128, 8], F32))
        s1s2 = en(nc.sbuf_tensor("s1s2", [128, 32], F32))
        mvar = en(nc.sbuf_tensor("mvar", [128, 16], F32))
        rstd_sb = en(nc.sbuf_tensor("rstd_sb", [128, 16], F32))
        nb_sb = en(nc.sbuf_tensor("nb_sb", [128, 16], F32))
        z_sb = en(nc.sbuf_tensor("z_sb", [128, 16], F32))
        gate_sb = en(nc.sbuf_tensor("gate_sb", [128, 128], F32))
        gsc = en(nc.sbuf_tensor("gsc", [128, 8], F32))
        mcol = en(nc.sbuf_tensor("mcol", [128, 4], F32))
        negm = en(nc.sbuf_tensor("negm", [128, 1], F32))
        dencol = en(nc.sbuf_tensor("dencol", [128, 1], F32))
        rt_a = en(nc.sbuf_tensor("rt_a", [128, 8], F32))
        rt_b = en(nc.sbuf_tensor("rt_b", [128, 8], F32))
        wexp = en(nc.sbuf_tensor("wexp", [128, 128], F32))
        rb_row = en(nc.sbuf_tensor("rb_row", [1, 8], F32))
        ones_f = en(nc.sbuf_tensor("ones_f", [1, 128], F32))
        eps_col = en(nc.sbuf_tensor("eps_col", [128, 1], F32))
        xn_stage = en(nc.sbuf_tensor("xn_stage", [128, 4096], BF16))
        eye_b = en(nc.sbuf_tensor("eye_b", [128, 128], BF16))
        ones_b = en(nc.sbuf_tensor("ones_b", [128, 1], BF16))
        ones_row_b = en(nc.sbuf_tensor("ones_row_b", [1, 128], BF16))
        rb_row_b = en(nc.sbuf_tensor("rb_row_b", [1, 8], BF16))
        ksum_fm = en(nc.sbuf_tensor("ksum_fm", [128, 8], BF16))
        wg_sb = en(nc.sbuf_tensor("wg_sb", [128, 64], BF16))
        big1 = en(nc.psum_tensor("big1", [128, 1024], F32))
        big2 = en(nc.psum_tensor("big2", [128, 1024], F32))
        u1 = en(nc.psum_tensor("u1", [128, 512], F32))
        u2 = en(nc.psum_tensor("u2", [128, 512], F32))
        utr = en(nc.psum_tensor("utr", [128, 1024], F32))
        sem_pe = en(nc.semaphore("pe"))
        sem_act = en(nc.semaphore("act"))
        sem_dve = en(nc.semaphore("dve"))
        sems = {"pe": sem_pe, "act": sem_act, "dve": sem_dve}
        for base in ("sp", "wd"):
            for i in range(8):
                sems[f"{base}{i}"] = en(nc.semaphore(f"{base}{i}"))
        block = en(nc.Block())
        b_row = tmp_f32[0:1, :]

        xn_fm = AR[:, 0:16384]
        q_fm = AR[:, 16384:32768]
        k_tm = AR[:, 32768:49152]
        v_tm = AR[:, 49152:65536]
        wbuf = [AR[:, 65536:69632], AR[:, 69632:73728]]
        wbuf_names = ["wb0", "wb1"]
        attn_fm = AR[:, 0:16384]
        kv_sb = AR[:, 32768:40960]
        woutT_sb = AR[:, 40960:49152]
        su = AR[:, 0:22528]
        w3_sb = AR[:, 22528:45056]
        x2n = [AR[:, 49152:57344], AR[:, 57344:65536]]
        w12buf = wbuf

        psl = {"b1a": big1[:, 0:512], "b1b": big1[:, 512:1024],
               "b2a": big2[:, 0:512], "b2b": big2[:, 512:1024],
               "u1": u1[:, :], "u2": u2[:, :],
               "utr0": utr[:, 0:512], "utr1": utr[:, 512:1024]}

        def mm_group(out_ap, pairs, waits):
            n = len(pairs)
            for i, (l, r) in enumerate(pairs):
                S.op("PE",
                     (lambda e, o=out_ap, l=l, r=r, st=(i == 0), so=(i == n - 1):
                      e.matmul(o, l, r, start=st, stop=so, skip_group_check=True)),
                     waits=waits if i == 0 else (),
                     inc=("pe", 1) if i == n - 1 else None)
            return ("pe", S.cnt["pe"])

        def dma(eng, base, out_ap, in_ap, waits):
            i = S.dma_idx[base]
            S.dma_idx[base] = i + 1
            sem = f"{base}{i % 8}"
            return S.op(eng, (lambda e, o=out_ap, i=in_ap: e.dma_start(out=o, in_=i)),
                        waits=waits, inc=(sem, 16))

        c_eye = dma("POOL", "wd", eye_b[:, :], eye_d, [])
        c_brow = dma("SP", "sp", b_row, b_d, [])
        c_rbrow = dma("SP", "sp", rb_row[:, :], rb_d, [])
        wg3 = wg_d.rearrange("(c p) f -> p c f", p=128)
        c_wg = dma("POOL", "wd", wg_sb[:, :].rearrange("p (c f) -> p c f", c=8), wg3, [])
        c_ones = S.op("DVE", lambda e: e.memset(ones_f[:, :], 1.0), inc=("dve", 1))
        c_onesb = S.op("DVE", lambda e: e.memset(ones_b[:, :], 1.0), inc=("dve", 1))
        c_onesrb = S.op("DVE", lambda e: e.memset(ones_row_b[:, :], 1.0), inc=("dve", 1))
        c_eps = S.op("DVE", lambda e: e.memset(eps_col[:, :], EPS_LN), inc=("dve", 1))
        c_rbb = S.op("DVE", lambda e: e.tensor_copy(rb_row_b[:, :], rb_row[:, :]),
                     waits=[c_rbrow], inc=("dve", 1))

        g1 = mm_group(u1[:, :], [(ones_f[:, :], b_row[0:1, 0:512])], [c_brow, c_ones])
        g2 = mm_group(u2[:, :], [(ones_f[:, :], b_row[0:1, 512:1024])], [c_brow])
        S.rel(["tmp0", "tmp1"], g2)
        c_bb1 = S.op("DVE", lambda e: e.tensor_copy(b_bcast[:, 0:512], u1[:, :]),
                     waits=[g1], inc=("dve", 1))
        c_bb2 = S.op("DVE", lambda e: e.tensor_copy(b_bcast[:, 512:1024], u2[:, :]),
                     waits=[g2], inc=("dve", 1))
        g3 = mm_group(u1[:, 0:8], [(ones_f[:, :], rb_row[0:1, :])], [c_rbrow, c_bb1])
        c_rbbc = S.op("DVE", lambda e: e.tensor_copy(rb_bcast[:, :], u1[:, 0:8]),
                      waits=[g3], inc=("dve", 1))
        S.rel(["u1"], c_rbbc)
        S.rel(["u2"], c_bb2)
        ready_b = [c_bb1, c_bb2]
        ready_rbb = [c_rbbc]

        chain = {"ln_dve": None, "ln_act": None, "rt": None}

        def ln_chunk(src, xs_col, k, src_gate):
            mv = mvar[:, (xs_col % 4) * 4:(xs_col % 4) * 4 + 4]
            mv_gate = S.acq([f"mv{xs_col % 4}"])
            c_s1 = S.op("ACT", (lambda e, s=src, o=s1s2[:, xs_col:xs_col + 1]:
                                e.activation(tmp_f32[:, :], s, AF.Copy, accum_out=o)),
                        waits=src_gate + S.acq(["tmp0", "tmp1"]), inc=("act", 1))
            c_s2 = S.op("ACT", (lambda e, s=src, o=s1s2[:, 16 + xs_col:17 + xs_col]:
                                e.activation(tmp_f32[:, :], s, AF.Square, accum_out=o)),
                        waits=src_gate + [c_s1], inc=("act", 1))
            S.rel(["tmp0", "tmp1"], c_s2)
            c_m = S.op("DVE", (lambda e, i=s1s2[:, xs_col:xs_col + 1], o=mv[:, 0:1]:
                               e.tensor_scalar_mul(o, i, 1.0 / D)),
                       waits=[c_s1] + mv_gate, inc=("dve", 1))
            c_msq = S.op("DVE", (lambda e, o=mv[:, 1:2], i=mv[:, 0:1]:
                                 e.tensor_mul(o, i, i)),
                         waits=[c_m], inc=("dve", 1))
            c_var = S.op("DVE", (lambda e, i=s1s2[:, 16 + xs_col:17 + xs_col],
                                 o=mv[:, 2:3], m2=mv[:, 1:2]:
                                 e.scalar_tensor_tensor(o, i, 1.0 / D,
                                                        m2, ALU.mult, ALU.subtract)),
                         waits=[c_msq, c_s2], inc=("dve", 1))
            c_ln = S.op("ACT", (lambda e, o=mv[:, 3:4], i=mv[:, 2:3]:
                                e.activation(o, i, AF.Ln, bias=eps_col[:, :])),
                        waits=[c_var, c_eps], inc=("act", 1))
            c_rstd = S.op("ACT", (lambda e, o=rstd_sb[:, xs_col:xs_col + 1], i=mv[:, 3:4]:
                                  e.activation(o, i, AF.Exp, scale=-0.5)),
                          waits=[c_ln], inc=("act", 1))
            c_nb = S.op("DVE", (lambda e, o=nb_sb[:, xs_col:xs_col + 1], i=mv[:, 0:1]:
                                e.tensor_scalar_mul(o, i, -1.0)),
                        waits=[c_var], inc=("dve", 1))
            S.rel([f"mv{xs_col % 4}"], [c_rstd, c_nb])
            dst = xn_stage[:, k * 1024:(k + 1) * 1024]
            c_ap = S.op("DVE", (lambda e, s=src, d=dst,
                                r=rstd_sb[:, xs_col:xs_col + 1],
                                b=nb_sb[:, xs_col:xs_col + 1]:
                                e.tensor_scalar(d, s, b, r, ALU.add, ALU.mult)),
                        waits=[c_nb, c_rstd] + S.acq([f"xn{k}"]), inc=("dve", 1))
            return c_s1, c_ap

        def transpose_chunk(k, dst3_ap, c_apply, extra=()):
            gates = [c_apply, c_eye] + S.acq(["utr0", "utr1"]) + list(extra)
            g = None
            for c in range(8):
                src = xn_stage[:, k * 1024 + c * 128: k * 1024 + (c + 1) * 128]
                g = mm_group(utr[:, c * 128:(c + 1) * 128], [(src, eye_b[:, :])],
                             gates if c == 0 else [])
            c_cp = S.op("DVE", (lambda e, o=dst3_ap:
                                e.tensor_copy(o, utr[:, :].rearrange("p (c f) -> p c f", c=8))),
                        waits=[g], inc=("dve", 1))
            S.rel(["utr0", "utr1"], c_cp)
            return c_cp

        slots6 = ["u1", "u2", "b1a", "b1b", "b2a", "b2b"]
        sidx = [0]

        def tm_mms(dst_tm, tc, half, bslot, gates):
            sl = slots6[sidx[0] % 6]; sidx[0] += 1
            ps = psl[sl]
            pairs = [(xn_fm[:, c * 2048 + tc * 128: c * 2048 + (tc + 1) * 128],
                      wbuf[bslot][:, c * 512:(c + 1) * 512])
                     for c in range(8)]
            g = mm_group(ps, pairs, gates + S.acq([sl]))
            dst = dst_tm[:, tc * 1024 + half * 512: tc * 1024 + half * 512 + 512]

            def emit_evac(is_k):
                if is_k:
                    c_ev = S.op("DVE", (lambda e, o=dst, i_=ps:
                                        e.tensor_scalar(o, i_, 0.0, EPS_ATTN,
                                                        ALU.max, ALU.add)),
                                waits=[g] + S.acq(["ktm"]), inc=("dve", 1))
                else:
                    c_ev = S.op("ACT", (lambda e, o=dst, i_=ps:
                                        e.activation(o, i_, AF.Copy)),
                                waits=[g] + S.acq(["vtm"]), inc=("act", 1))
                S.rel([sl], c_ev)
                return c_ev
            return emit_evac

        kv_mm_last = None
        for b in range(NB):
            own = (b == NB - 1)
            c_k0 = dma("POOL", "wd", wbuf[0], wk_d[:, 0:4096], S.acq(["wb0"]))
            c_k1 = dma("POOL", "wd", wbuf[1], wk_d[:, 4096:8192], S.acq(["wb1"]))
            ln_last = [None] * NT
            k_ev, v_ev = [], []
            pend_ev = None
            for tc in range(NT):
                k = tc % 4
                xs = x_stage[:, (tc % 3) * 1024:(tc % 3 + 1) * 1024]
                c_x = dma("SP", "sp", xs,
                          x_d[b * TC + tc * 128: b * TC + (tc + 1) * 128, :],
                          S.acq([f"xs{tc % 3}"]))
                c_s2, c_ap = ln_chunk(xs, tc, k, [c_x])
                S.rel([f"xs{tc % 3}"], [c_ap, c_s2])
                if pend_ev is not None:
                    k_ev += [f(True) for f in pend_ev]
                dst3 = xn_fm.rearrange("p (c t) -> p c t", c=8)[:, :, tc * 128:(tc + 1) * 128]
                c_last = transpose_chunk(k, dst3, c_ap, extra=S.acq(["xnfm"]))
                S.rel([f"xn{k}"], c_last)
                ln_last[tc] = c_last
                pend_ev = [tm_mms(k_tm, tc, 0, 0, [c_k0, c_last]),
                           tm_mms(k_tm, tc, 1, 1, [c_k1, c_last])]
            k_ev += [f(True) for f in pend_ev]
            pe_sweep1 = ("pe", S.cnt["pe"])
            c_v0 = dma("POOL", "wd", wbuf[0], wv_d[:, 0:4096], [pe_sweep1])
            c_v1 = dma("POOL", "wd", wbuf[1], wv_d[:, 4096:8192], [pe_sweep1])
            pend_ev = None
            for tc in range(NT):
                if pend_ev is not None:
                    v_ev += [f(False) for f in pend_ev]
                pend_ev = [tm_mms(v_tm, tc, 0, 0, [c_v0, ln_last[tc]]),
                           tm_mms(v_tm, tc, 1, 1, [c_v1, ln_last[tc]])]
            v_ev += [f(False) for f in pend_ev]
            S.rel(["xnfm"] if not own else [], ("pe", S.cnt["pe"]))
            pe_sweep2 = ("pe", S.cnt["pe"])
            S.rel(["wb0", "wb1"], pe_sweep2)

            kready = [_mx(k_ev), _mx(v_ev), c_onesb]
            for dc in range(NDC):
                bign = ["b1a", "b1b"] if dc % 2 == 0 else ["b2a", "b2b"]
                bigps = big1 if dc % 2 == 0 else big2
                g_eh = []
                for eh in range(2):
                    pairs = [(k_tm[:, tc * 1024 + dc * 128: tc * 1024 + (dc + 1) * 128],
                              v_tm[:, tc * 1024 + eh * 512: tc * 1024 + (eh + 1) * 512])
                             for tc in range(NT)]
                    g_eh.append(mm_group(bigps[:, eh * 512:(eh + 1) * 512], pairs,
                                         kready + S.acq([bign[eh]])))
                usl = "u1" if dc % 2 == 0 else "u2"
                ups = psl[usl]
                pairs = [(k_tm[:, tc * 1024 + dc * 128: tc * 1024 + (dc + 1) * 128],
                          ones_b[:, :]) for tc in range(NT)]
                g_ks = mm_group(ups[:, 0:1], pairs, kready + S.acq([usl]))
                kvslice = x2t[:, dc * 1025: dc * 1025 + 1024]
                ksslice = x2t[:, dc * 1025 + 1024: dc * 1025 + 1025]
                if b == 0:
                    c_e1 = S.op("DVE", (lambda e, o=kvslice, i_=bigps[:, :]:
                                        e.tensor_copy(o, i_)),
                                waits=g_eh, inc=("dve", 1))
                    c_e2 = S.op("DVE", (lambda e, o=ksslice, i_=ups[:, 0:1]:
                                        e.tensor_copy(o, i_)),
                                waits=[g_ks], inc=("dve", 1))
                else:
                    c_e1 = S.op("DVE", (lambda e, o=kvslice, i_=bigps[:, :]:
                                        e.tensor_add(o, o, i_)),
                                waits=g_eh, inc=("dve", 1))
                    c_e2 = S.op("DVE", (lambda e, o=ksslice, i_=ups[:, 0:1]:
                                        e.tensor_add(o, o, i_)),
                                waits=[g_ks], inc=("dve", 1))
                S.rel(bign, c_e1)
                S.rel([usl], c_e2)
                kv_mm_last = ("pe", S.cnt["pe"])
            S.rel(["ktm", "vtm"], kv_mm_last)
        kv_done_pe = ("pe", S.cnt["pe"])

        c_q0 = dma("POOL", "wd", wbuf[0], wq_d[:, 0:4096], S.acq(["wb0"]))
        c_q1 = dma("POOL", "wd", wbuf[1], wq_d[:, 4096:8192], S.acq(["wb1"]))
        q_ev = []
        for pname, c_pan in (("q0", c_q0), ("q1", c_q1)):
            bslot = 0 if pname == "q0" else 1
            for fq_rel in range(4):
                fq = (0 if pname == "q0" else 4) + fq_rel
                for ts in range(4):
                    sl = slots6[sidx[0] % 6]; sidx[0] += 1
                    ps = psl[sl]
                    pairs = [(wbuf[bslot][:, c * 512 + fq_rel * 128: c * 512 + (fq_rel + 1) * 128],
                              xn_fm[:, c * 2048 + ts * 512: c * 2048 + (ts + 1) * 512])
                             for c in range(8)]
                    g = mm_group(ps, pairs, [c_pan, _mx(ln_last)] + S.acq([sl]))
                    dstq = q_fm[:, fq * 2048 + ts * 512: fq * 2048 + (ts + 1) * 512]
                    c_ev = S.op("DVE", (lambda e, o=dstq, i_=ps:
                                        e.tensor_scalar(o, i_, 0.0, EPS_ATTN,
                                                        ALU.max, ALU.add)),
                                waits=[g], inc=("dve", 1))
                    S.rel([sl], c_ev)
                    q_ev.append(c_ev)
        ready_q = [_mx(q_ev)]
        pe_after_q = ("pe", S.cnt["pe"])
        S.rel(["wb0", "wb1"], pe_after_q)
        S.rel(["xnfm"], pe_after_q)

        c_cast = S.op("DVE",
                      (lambda e: e.tensor_copy(
                          kv_sb.rearrange("p (c f) -> p c f", c=8),
                          x2t[:, 0:8200].rearrange("p (c f) -> p c f", f=1025)[:, :, 0:1024])),
                      waits=[kv_done_pe], inc=("dve", 1))
        c_kscast = S.op("DVE",
                        (lambda e: e.tensor_copy(
                            ksum_fm[:, :].rearrange("p (c f) -> p c f", f=1),
                            x2t[:, 0:8200].rearrange("p (c f) -> p c f", f=1025)[:, :, 1024:1025])),
                        waits=[kv_done_pe], inc=("dve", 1))
        S.rel(["x2t"], [c_cast, c_kscast])
        c_wout = dma("POOL", "wd", woutT_sb, wout_d, [kv_done_pe])

        z_parts = []
        for tc in range(NT):
            usl = "u1" if tc % 2 == 0 else "u2"
            ups = psl[usl]
            pairs = [(q_fm[:, c * 2048 + tc * 128: c * 2048 + (tc + 1) * 128],
                      ksum_fm[:, c:c + 1])
                     for c in range(8)]
            g = mm_group(ups[:, 0:1], pairs, [c_kscast] + ready_q + S.acq([usl]))
            c_z = S.op("DVE", (lambda e, o=z_sb[:, tc:tc + 1], i_=ups[:, 0:1]:
                               e.tensor_scalar_add(o, i_, EPS_ATTN)),
                       waits=[g], inc=("dve", 1))
            S.rel([usl], c_z)
            z_parts.append(c_z)
        c_zr = S.op("DVE", lambda e: e.reciprocal(z_sb[:, :], z_sb[:, :]),
                    waits=[_mx(z_parts)], inc=("dve", 1))
        ready_z = [c_zr]

        attn_ev = []
        slots4 = ["b1a", "b1b", "b2a", "b2b"]
        sj = 0
        for ec in range(8):
            for ts in range(4):
                sl = slots4[sj % 4]; sj += 1
                ps = psl[sl]
                pairs = [(kv_sb[:, c * 1024 + ec * 128: c * 1024 + (ec + 1) * 128],
                          q_fm[:, c * 2048 + ts * 512: c * 2048 + (ts + 1) * 512])
                         for c in range(8)]
                g = mm_group(ps, pairs, [c_cast] + ready_q + S.acq([sl]))
                dst = attn_fm[:, ec * 2048 + ts * 512: ec * 2048 + (ts + 1) * 512]
                c_ev = S.op("ACT", (lambda e, o=dst, i_=ps: e.activation(o, i_, AF.Copy)),
                            waits=[g], inc=("act", 1))
                S.rel([sl], c_ev)
                attn_ev.append(c_ev)
        ready_attn = [_mx(attn_ev)]

        x2_store = {}
        t_fill = [dict(), dict()]
        ln2_last = [[], []]
        pend_tr = []

        def flush_tr(keep=0):
            while len(pend_tr) > keep:
                k, dst3, c_ap, tT = pend_tr.pop(0)
                c_last = transpose_chunk(k, dst3, c_ap, extra=[kv_done_pe])
                S.rel([f"xn{k}"], c_last)
                ln2_last[tT].append(c_last)
        for tc in range(NT):
            T, tc8 = tc // 8, tc % 8
            k = tc % 4
            xs = x_stage[:, (tc % 3) * 1024:(tc % 3 + 1) * 1024]
            c_x = dma("SP", "sp", xs,
                      x_d[(NB - 1) * TC + tc * 128: (NB - 1) * TC + (tc + 1) * 128, :],
                      S.acq([f"xs{tc % 3}"]))
            c_add = S.op("DVE", (lambda e, o=xs: e.tensor_add(o, o, b_bcast[:, :])),
                         waits=[c_x] + ready_b, inc=("dve", 1))
            bign = ["b1a", "b1b"] if tc % 2 == 0 else ["b2a", "b2b"]
            bigps = big1 if tc % 2 == 0 else big2
            gs = []
            for fh in range(2):
                pairs = [(attn_fm[:, c * 2048 + tc * 128: c * 2048 + (tc + 1) * 128],
                          woutT_sb[:, c * 1024 + fh * 512: c * 1024 + (fh + 1) * 512])
                         for c in range(8)]
                gs.append(mm_group(bigps[:, fh * 512:(fh + 1) * 512], pairs,
                                   [c_wout] + ready_attn + S.acq([bign[fh]])))
            if tc < 8:
                dst = x2t[:, tc * 1024:(tc + 1) * 1024]
                c_ev = S.op("DVE", (lambda e, o=dst, i_=bigps[:, :], z=z_sb[:, tc:tc + 1],
                                    xi=xs: e.scalar_tensor_tensor(o, i_, z, xi,
                                                                  ALU.mult, ALU.add)),
                            waits=gs + [c_add] + ready_z + S.acq(["x2t"]), inc=("dve", 1))
                S.rel(bign, c_ev)
                S.rel([f"xs{tc % 3}"], c_ev)
                t_fill[0][tc8] = c_ev
                src_ln = dst
                ln_gate = [c_ev]
            else:
                c_ev = S.op("DVE", (lambda e, o=xs, i_=bigps[:, :], z=z_sb[:, tc:tc + 1]:
                                    e.scalar_tensor_tensor(o, i_, z, o, ALU.mult, ALU.add)),
                            waits=gs + [c_add] + ready_z, inc=("dve", 1))
                S.rel(bign, c_ev)
                src_ln = xs
                ln_gate = [c_ev]
            c_s2, c_ap = ln_chunk(src_ln, tc, k, ln_gate)
            if tc >= 8:
                c_st = dma("SP", "sp", x2scr_d[tc8 * 128:(tc8 + 1) * 128, :], xs, [c_ev])
                S.rel([f"xs{tc % 3}"], [c_st, c_ap, c_s2])
                x2_store[tc8] = c_st
            flush_tr(keep=1)
            pend_tr.append((k,
                            x2n[T].rearrange("p (c t) -> p c t", c=8)[:, :, tc8 * 128:(tc8 + 1) * 128],
                            c_ap, T))
        flush_tr()
        pe_after_a4 = ("pe", S.cnt["pe"])
        S.rel(["su_war"], pe_after_a4)
        S.rel(["w3"], pe_after_a4)
        ready_x2n_t = [[_mx(ln2_last[0])], [_mx(ln2_last[1])]]

        ready_wexp_t = [None, None]
        for gtc in range(16):
            T, tc8 = gtc // 8, gtc % 8
            usl = "u1" if gtc % 2 == 0 else "u2"
            ups = psl[usl]
            pairs = [(x2n[T][:, c * 1024 + tc8 * 128: c * 1024 + (tc8 + 1) * 128],
                      wg_sb[:, c * 8:(c + 1) * 8]) for c in range(8)]
            g = mm_group(ups[:, 0:8], pairs,
                         [c_wg] + ready_x2n_t[T] + S.acq([usl]))
            graw = gate_sb[:, gtc * 8:(gtc + 1) * 8]
            c_g = S.op("ACT", (lambda e, o=graw, i_=ups[:, 0:8]:
                               e.activation(o, i_, AF.Copy)),
                       waits=[g], inc=("act", 1))
            S.rel([usl], c_g)
            gw = gsc[:, 0:8]
            gcol = gsc[:, 64 + (gtc % 8) * 8: 64 + (gtc % 8) * 8 + 8] if False else None
            prev_rt = [chain["rt"]] if chain["rt"] else []
            gcol = gate_sb[:, gtc * 8:(gtc + 1) * 8]
            c = S.op("DVE", (lambda e, o=gcol, i_=graw: e.tensor_add(o, i_, rb_bcast[:, :])),
                     waits=[c_g] + ready_rbb + prev_rt, inc=("dve", 1))
            c = S.op("DVE", (lambda e, g_=gcol: e.reduce_max(mcol[:, 0:1], g_, AX.X)),
                     waits=[c], inc=("dve", 1))
            c = S.op("DVE", (lambda e, g_=gcol: e.tensor_single_scalar(
                rt_a[:, :], g_, mcol[:, 0:1], ALU.is_equal)), waits=[c], inc=("dve", 1))
            c = S.op("DVE", (lambda e, g_=gcol, o=gw: e.scalar_tensor_tensor(
                o, rt_a[:, :], -1e30, g_, ALU.mult, ALU.add)), waits=[c], inc=("dve", 1))
            for step in (1, 2):
                c = S.op("DVE", (lambda e, o=mcol[:, step:step + 1], i_=gw:
                                 e.reduce_max(o, i_, AX.X)), waits=[c], inc=("dve", 1))
                c = S.op("DVE", (lambda e, i_=gw, m=mcol[:, step:step + 1]:
                                 e.tensor_single_scalar(rt_a[:, :], i_, m, ALU.is_equal)),
                         waits=[c], inc=("dve", 1))
                c = S.op("DVE", (lambda e, o=gw: e.scalar_tensor_tensor(
                    o, rt_a[:, :], -1e30, o, ALU.mult, ALU.add)), waits=[c], inc=("dve", 1))
            c = S.op("DVE", (lambda e, i_=gw: e.reduce_max(mcol[:, 3:4], i_, AX.X)),
                     waits=[c], inc=("dve", 1))
            c = S.op("DVE", lambda e: e.tensor_scalar_mul(negm[:, :], mcol[:, 0:1], -1.0),
                     waits=[c], inc=("dve", 1))
            c_exp = S.op("ACT", (lambda e, g_=gcol: e.activation(
                rt_b[:, :], g_, AF.Exp, bias=negm[:, :])), waits=[c], inc=("act", 1))
            c = S.op("DVE", (lambda e, g_=gcol: e.tensor_single_scalar(
                rt_a[:, :], g_, mcol[:, 3:4], ALU.is_ge)), waits=[c_exp], inc=("dve", 1))
            c = S.op("DVE", lambda e: e.tensor_mul(rt_b[:, :], rt_b[:, :], rt_a[:, :]),
                     waits=[c], inc=("dve", 1))
            c = S.op("DVE", lambda e: e.reduce_sum(dencol[:, :], rt_b[:, :], AX.X),
                     waits=[c], inc=("dve", 1))
            c = S.op("DVE", lambda e: e.reciprocal(dencol[:, :], dencol[:, :]),
                     waits=[c], inc=("dve", 1))
            wcol = wexp[:, gtc * 8:(gtc + 1) * 8]
            c_w = S.op("DVE", (lambda e, o=wcol: e.tensor_scalar_mul(
                o, rt_b[:, :], dencol[:, :])), waits=[c], inc=("dve", 1))
            chain["rt"] = c_w
            if tc8 == 7:
                ready_wexp_t[T] = [c_w]

        y_slice_store = {}
        for T in range(2):
            ready_x2n = ready_x2n_t[T]
            ready_wexp = ready_wexp_t[T]
            x2nT = x2n[T]
            if T == 1:
                for ts8 in range(NTT):
                    c_l = dma("SP", "sp",
                              x2t[:, ts8 * 1024:(ts8 + 1) * 1024],
                              x2scr_d[ts8 * 128:(ts8 + 1) * 128, :],
                              [y_slice_store[ts8], x2_store[ts8]])
                    t_fill[1][ts8] = c_l
            eo_last_by_slice = {}
            for ex in range(NEX):
                c_g12 = {}

                def load_w12(g_, ex=ex):
                    bslot = g_ % 2
                    return dma("POOL", "wd", w12buf[bslot], w12_d[ex, g_],
                               S.acq([wbuf_names[bslot]]))

                c_g12[0] = load_w12(0)
                c_g12[1] = load_w12(1)
                c_w3_all = dma("POOL", "wd", w3_sb, w3_d[ex], S.acq(["w3"]))

                su_ev = []
                ui = 0
                for g_ in range(NG):
                    bslot = g_ % 2
                    if g_ not in c_g12:
                        c_g12[g_] = load_w12(g_)
                    for jj in (0, 1):
                        j = 2 * g_ + jj
                        for ts in range(2):
                            names = [("u1", "u2"), ("b1a", "b1b"),
                                     ("utr0", "utr1")][ui % 3]
                            ui += 1
                            ps_h1, ps_h2 = psl[names[0]], psl[names[1]]
                            w1l = [w12buf[bslot][:, c * 512 + jj * 128: c * 512 + (jj + 1) * 128]
                                   for c in range(8)]
                            w2l = [w12buf[bslot][:, c * 512 + 256 + jj * 128:
                                                 c * 512 + 256 + (jj + 1) * 128]
                                   for c in range(8)]
                            rhs = [x2nT[:, c * 1024 + ts * 512: c * 1024 + (ts + 1) * 512]
                                   for c in range(8)]
                            gw1 = [c_g12[g_]] + ready_x2n + S.acq([names[0]])
                            g1_ = mm_group(ps_h1, list(zip(w1l, rhs)), gw1)
                            g2_ = mm_group(ps_h2, list(zip(w2l, rhs)), S.acq([names[1]]))
                            dst_su = su[:, j * 1024 + ts * 512: j * 1024 + (ts + 1) * 512]
                            c_si = S.op("ACT", (lambda e, o=dst_su, i_=ps_h1:
                                                e.activation(o, i_, AF.Silu)),
                                        waits=[g1_] + S.acq(["su_war"]), inc=("act", 1))
                            c_su = S.op("DVE", (lambda e, o=dst_su, i_=ps_h2:
                                                e.tensor_mul(o, o, i_)),
                                        waits=[c_si, g2_], inc=("dve", 1))
                            S.rel([names[0]], c_si)
                            S.rel([names[1]], c_su)
                            su_ev.append(c_su)
                    S.rel([wbuf_names[bslot]], ("pe", S.cnt["pe"]))
                ready_su = [_mx(su_ev)]

                for ts8 in range(NTT):
                    bign = ["b2a", "b2b"] if ts8 % 2 == 0 else ["b1a", "b1b"]
                    bigps = big2 if ts8 % 2 == 0 else big1
                    tsq = ts8 // 4
                    first_w = [c_w3_all, su_ev[tsq]] + S.acq(bign)
                    n_mm = 0
                    for j in range(NHC):
                        for dh in range(2):
                            st = (j == 0)
                            so = (j == NHC - 1)
                            if n_mm == 0:
                                w = first_w
                            elif j == 11 and dh == 0:
                                w = [su_ev[22 + tsq]]
                            elif j == NHC - 1 and dh == 0:
                                w = ready_su
                            else:
                                w = ()
                            S.op("PE",
                                 (lambda e, o=bigps[:, dh * 512:(dh + 1) * 512],
                                  l=su[:, j * 1024 + ts8 * 128: j * 1024 + (ts8 + 1) * 128],
                                  r=w3_sb[:, j * 1024 + dh * 512: j * 1024 + (dh + 1) * 512],
                                  st=st, so=so:
                                  e.matmul(o, l, r, start=st, stop=so, skip_group_check=True)),
                                 waits=w,
                                 inc=("pe", 1) if (so and dh == 1) else None)
                            n_mm += 1
                    g_eo = ("pe", S.cnt["pe"])
                    dst = x2t[:, ts8 * 1024:(ts8 + 1) * 1024]
                    fillw = [t_fill[T][ts8]]
                    if ex == 0:
                        c_ev = S.op("DVE", (lambda e, o=dst, i_=bigps[:, :]:
                                            e.scalar_tensor_tensor(o, i_, 1.0, o,
                                                                   ALU.mult, ALU.add)),
                                    waits=[g_eo] + fillw, inc=("dve", 1))
                    else:
                        wcol = wexp[:, (T * 8 + ts8) * 8 + (ex - 1):
                                    (T * 8 + ts8) * 8 + ex]
                        c_ev = S.op("DVE", (lambda e, o=dst, i_=bigps[:, :], w=wcol:
                                            e.scalar_tensor_tensor(o, i_, w, o,
                                                                   ALU.mult, ALU.add)),
                                    waits=[g_eo] + ready_wexp + fillw, inc=("dve", 1))
                    S.rel(bign, c_ev)
                    eo_last_by_slice[ts8] = c_ev
                pe_now = ("pe", S.cnt["pe"])
                S.rel(["su_war"], pe_now)
                S.rel(["w3"], pe_now)

            for ts8 in range(NTT):
                c_y = dma("SP", "sp",
                          y_d[T * 1024 + ts8 * 128: T * 1024 + (ts8 + 1) * 128, :],
                          x2t[:, ts8 * 1024:(ts8 + 1) * 1024],
                          [eo_last_by_slice[ts8]])
                y_slice_store[ts8] = c_y

        S.op("SP", lambda e: e.nop(),
             waits=[(f"sp{i}", S.cnt[f"sp{i}"]) for i in range(8)
                    if S.cnt[f"sp{i}"] > 0])
        S.emit(block, sems)
    return nc, {e: len(S.q[e]) for e in ENGINES}


_NC_CACHE = {}


def _get_nc():
    if "nc" not in _NC_CACHE:
        _NC_CACHE["nc"] = build_nc()
    return _NC_CACHE["nc"]


def kernel(L0_attn_x, L0_attn_W_qkv, L0_attn_W_out, L0_attn_b_out,
           L0_ffn_Wg, L0_ffn_routing_bias, L0_ffn_W1, L0_ffn_W2, L0_ffn_W3,
           L0_ffn_Ws1, L0_ffn_Ws2, L0_ffn_Ws3):
    x = np.asarray(L0_attn_x, dtype=np.float32)
    B, N, D_ = x.shape
    x_flat = x.reshape(B * N, D_)

    wqkvT = np.asarray(L0_attn_W_qkv, np.float32).T

    def _pack_halves(blk):
        return np.ascontiguousarray(
            blk.reshape(8, 128, 2, 512).transpose(2, 1, 0, 3).reshape(2, 128, 4096)
            .transpose(1, 0, 2).reshape(128, 8192))

    wqP = _pack_halves(wqkvT[:, 0:1024])
    wkP = _pack_halves(wqkvT[:, 1024:2048])
    wvP = _pack_halves(wqkvT[:, 2048:3072])
    woutT = np.asarray(L0_attn_W_out, np.float32).T
    woutP = np.ascontiguousarray(
        woutT.reshape(8, 128, 1024).transpose(1, 0, 2).reshape(128, 8192))
    wgT = np.ascontiguousarray(np.asarray(L0_ffn_Wg, np.float32).T)
    b_out = np.ascontiguousarray(np.asarray(L0_attn_b_out, np.float32).reshape(1, D_))
    rb = np.ascontiguousarray(np.asarray(L0_ffn_routing_bias, np.float32).reshape(1, 8))

    W1 = np.asarray(L0_ffn_W1, np.float32)
    W2 = np.asarray(L0_ffn_W2, np.float32)
    W3 = np.asarray(L0_ffn_W3, np.float32)
    Ws1 = np.asarray(L0_ffn_Ws1, np.float32)
    Ws2 = np.asarray(L0_ffn_Ws2, np.float32)
    Ws3 = np.asarray(L0_ffn_Ws3, np.float32)

    Hr = W1.shape[1]
    W1T = np.zeros((NEX, D_, H), np.float32)
    W2T = np.zeros((NEX, D_, H), np.float32)
    W1T[0, :, :Hr] = Ws1.T
    W2T[0, :, :Hr] = Ws2.T
    W1T[1:, :, :Hr] = W1.transpose(0, 2, 1)
    W2T[1:, :, :Hr] = W2.transpose(0, 2, 1)
    w12 = np.empty((NEX, D_, NG, 512), np.float32)
    for g in range(NG):
        j0, j1 = 2 * g * 128, (2 * g + 1) * 128
        w12[:, :, g, 0:128] = W1T[:, :, j0:j0 + 128]
        w12[:, :, g, 128:256] = W1T[:, :, j1:j1 + 128]
        w12[:, :, g, 256:384] = W2T[:, :, j0:j0 + 128]
        w12[:, :, g, 384:512] = W2T[:, :, j1:j1 + 128]
    w12 = np.ascontiguousarray(
        w12.reshape(NEX, 8, 128, NG, 512).transpose(0, 3, 2, 1, 4).reshape(NEX, NG, 128, 4096))

    w3T = np.zeros((NEX, H, D_), np.float32)
    w3T[0, :Hr, :] = Ws3.T
    w3T[1:, :Hr, :] = W3.transpose(0, 2, 1)
    w3T = np.ascontiguousarray(
        w3T.reshape(NEX, 22, 128, 1024).transpose(0, 2, 1, 3).reshape(NEX, 128, 22528))

    eye = np.eye(128, dtype=np.float32)

    nc, _counts = _get_nc()
    in_maps = []
    for c in range(NCORES):
        g, r = c // 4, c % 4
        batch = x_flat[g * NB * TC:(g + 1) * NB * TC]
        order = [(r + j) % 4 for j in range(1, 4)] + [r]
        xb = np.ascontiguousarray(
            np.concatenate([batch[blk * TC:(blk + 1) * TC] for blk in order], axis=0))
        in_maps.append({
            "x": xb,
            "wqP": wqP, "wkP": wkP, "wvP": wvP, "woutP": woutP, "wgT": wgT,
            "b_out": b_out, "rbias": rb,
            "w12P": w12, "w3P": w3T, "eye": eye,
        })
    res = run_bass_kernel_spmd(nc, in_maps, core_ids=list(range(NCORES)))
    out = np.concatenate([np.asarray(r["y"]) for r in res.results], axis=0)
    return out.reshape(B, N, D_).astype(np.float32)


if __name__ == "__main__":
    import time
    t0 = time.time()
    nc, counts = build_nc()
    print(f"build {time.time()-t0:.1f}s, instruction queue sizes:", counts,
          "total:", sum(counts.values()))
